# version 31
# baseline (speedup 1.0000x reference)
import numpy as np
import concourse.bass as bass
import concourse.mybir as mybir
from concourse.bass_utils import run_bass_kernel_spmd

F32 = mybir.dt.float32
BF16 = mybir.dt.bfloat16
U8 = mybir.dt.uint8
FP8 = mybir.dt.float8e4
AX = mybir.AxisListType.X
ALU = mybir.AluOpType
ACT = mybir.ActivationFunctionType

ENGS = ("sync", "scalar", "vector", "tensor", "gpsimd")

B, T, C = 4, 4096, 1024
TC = 512
NCH = T // TC
SP = TC + 2


class Sched:

    def __init__(self, nc):
        self.nc = nc
        self.ops = {e: [] for e in ENGS}
        self.count = {}
        self.sems = {}
        self.dep_targets = set()

    def _chan(self, chan):
        if chan not in self.sems:
            self.sems[chan] = self.nc.alloc_semaphore(
                f"sem_{chan[0]}_{chan[1]}")
            self.count[chan] = 0
        return chan

    def op(self, eng, emit, deps=(), key=None, inc=1):
        chan = self._chan((eng, "cmp") if key is None else ("dma", key))
        deps = [d for d in deps if d is not None]
        for d in deps:
            self.dep_targets.add(d)
        self.count[chan] += inc
        tok = (chan, self.count[chan])
        self.ops[eng].append((emit, deps, tok))
        return tok

    def emit(self):
        with self.nc.Block() as block:
            for eng_name in ENGS:
                if not self.ops[eng_name]:
                    continue

                def body(engobj, eng_name=eng_name):
                    waited = {}
                    last_inc = {}
                    for emit_fn, deps, tok in self.ops[eng_name]:
                        byc = {}
                        for dchan, dv in deps:
                            byc[dchan] = max(byc.get(dchan, 0), dv)
                        for dchan, dv in sorted(byc.items(), key=str):
                            if dv > waited.get(dchan, 0):
                                engobj.wait_ge(self.sems[dchan], dv)
                                waited[dchan] = dv
                        inst = emit_fn(engobj)
                        chan, cum = tok
                        if tok in self.dep_targets:
                            inst.then_inc(self.sems[chan],
                                          cum - last_inc.get(chan, 0))
                            last_inc[chan] = cum

                getattr(block, eng_name)(body)


class Buf:

    def __init__(self, name):
        self.name = name
        self.w = None
        self.r = []


def bcol(t, col, n, freew=4):
    return bass.AP(t, col, [[freew, 128], [0, n]])


def build_nc():
    nc = bass.Bass("TRN2", target_bir_lowering=False)
    s = Sched(nc)

    xT = nc.dram_tensor("xT", [C, T], F32, kind="ExternalInput")
    x_tm = nc.dram_tensor("x_tm", [T // 2, C], F32, kind="ExternalInput")
    wk_in = nc.dram_tensor("wk_in", [128, 4096], F32, kind="ExternalInput")
    wv_in = nc.dram_tensor("wv_in", [128, 4096], F32, kind="ExternalInput")
    wr_in = nc.dram_tensor("wr_in", [128, 4096], F32, kind="ExternalInput")
    wo_in = nc.dram_tensor("wo_in", [128, 8192], F32, kind="ExternalInput")
    w8_in = nc.dram_tensor("w8_in", [128, 8192], U8, kind="ExternalInput")
    tmk_in = nc.dram_tensor("tmk_in", [128, 8], F32, kind="ExternalInput")
    tmv_in = nc.dram_tensor("tmv_in", [128, 8], F32, kind="ExternalInput")
    tmr_in = nc.dram_tensor("tmr_in", [128, 8], F32, kind="ExternalInput")
    td_in = nc.dram_tensor("td_in", [128, 4], F32, kind="ExternalInput")
    tf_in = nc.dram_tensor("tf_in", [128, 4], F32, kind="ExternalInput")
    ws_in = nc.dram_tensor("ws_in", [128, 1], F32, kind="ExternalInput")
    bias_in = nc.dram_tensor("bias_in", [1, C], F32, kind="ExternalInput")
    y_out = nc.dram_tensor("y_out", [T // 2, C], F32, kind="ExternalOutput")
    hs_in = nc.dram_tensor("hs_in", [1, 3], mybir.dt.int32,
                           kind="ExternalInput")
    cc_big = nc.dram_tensor("cc_big", [8 * 512, TC], BF16)
    cc_out_all = nc.dram_tensor("cc_out_all", [8 * 1024, TC], BF16)

    A = nc.alloc_sbuf_tensor
    wkbf = A("wkbf", [128, 4096], BF16)
    wvbf = A("wvbf", [128, 4096], BF16)
    wrbf = A("wrbf", [128, 4096], BF16)
    wobf = A("wobf", [128, 8192], BF16)
    w8bf = A("w8bf", [128, 8192], BF16)
    ident = A("ident", [128, 128], BF16)
    ones_bf = A("ones_bf", [128, 128], BF16)
    xbf = [A("xbf0", [128, 8 * SP], BF16), A("xbf1", [128, 8 * SP], BF16)]
    wk2 = A("wk2", [128, 4096], BF16)
    wk2b = A("wk2b", [128, 4096], BF16)
    wv2 = A("wv2", [128, 4096], BF16)
    wv2b = A("wv2b", [128, 4096], BF16)
    wr2 = A("wr2", [128, 4096], BF16)
    wr2b = A("wr2b", [128, 4096], BF16)
    lnscr = A("lnscr", [128, 512], F32)
    rwbt = A("rwbt", [128, 2048], BF16)
    Ab = A("Ab", [128, 2048], BF16)
    Bb = A("Bb", [128, 2048], F32)
    E1 = A("E1", [128, 2048], F32)
    Cb = A("Cb", [128, 4 * (TC + 1)], F32)
    Db = A("Db", [128, 4 * (TC + 1)], F32)
    xtmb = A("xtmb", [128, 2 * C], F32)
    rwg = A("rwg", [128, 4096], BF16)
    zb = A("zb", [128, 2048], F32)
    x2b = A("x2b", [128, 2048], F32)
    qbfb = A("qbfb", [128, 2048], BF16)
    qfmb = A("qfmb", [128, 2048], BF16)
    invc = A("invc", [128, 2], BF16)
    biasbf = A("biasbf", [1, C], BF16)
    hselt = A("hselt", [1, 3], mybir.dt.int32)
    tmkv = A("tmkv", [128, 8], F32)
    tmvv = A("tmvv", [128, 8], F32)
    tmrv = A("tmrv", [128, 8], F32)
    tdv = A("tdv", [128, 4], F32)
    tfv = A("tfv", [128, 4], F32)
    wsv = A("wsv", [128, 1], F32)
    ew = A("ew", [128, 4], F32)
    eu = A("eu", [128, 4], F32)
    ruu = A("ruu", [128, 4], F32)
    w2 = A("w2", [128, 1], F32)
    tia = A("tia", [128, 4], F32)
    tib = A("tib", [128, 4], F32)
    mpr = [A("m0", [128, 4], F32), A("m1", [128, 4], F32)]
    kmx = A("kmx", [128, 4], F32)
    ta4 = A("ta4", [128, 4], F32)
    negm = A("negm", [128, 4], F32)
    dm4 = A("dm4", [128, 4], F32)
    rsc = A("rsc", [128, 4], F32)
    cgv = A("cgv", [128, 4], F32)
    chv = A("chv", [128, 4], F32)
    cig = A("cig", [128, 4], F32)
    cih = A("cih", [128, 4], F32)
    s1v = A("s1v", [128, 2], F32)
    s1v4 = A("s1v4", [128, 4], F32)
    s2v = A("s2v", [128, 2], F32)
    mea = A("mea", [128, 2], F32)
    ex2 = A("ex2", [128, 2], F32)
    msq = A("msq", [128, 2], F32)
    varv = A("varv", [128, 2], F32)
    lnv = A("lnv", [128, 2], F32)
    istd = A("istd", [128, 2], F32)
    ngm = A("ngm", [128, 2], F32)
    amx = A("amx", [128, 2], F32)
    xsv = A("xsv", [128, 2], F32)
    i2x = A("i2x", [128, 2], F32)
    stv = A("stv", [128, 2], F32)
    invs = A("invs", [128, 2], F32)
    tm1k = A("tm1k", [128, 8], F32)
    tm1v = A("tm1v", [128, 8], F32)
    tm1r = A("tm1r", [128, 8], F32)

    srb = Db[:].bitcast(BF16)[:, 2048:4096]
    q8b = qfmb[:].bitcast(FP8)[:, 0:2048]
    qinv = qbfb
    pk = [nc.alloc_psum_tensor("pk0", [128, 512], F32),
          nc.alloc_psum_tensor("pk1", [128, 512], F32)]
    pv = [nc.alloc_psum_tensor("pv0", [128, 512], F32),
          nc.alloc_psum_tensor("pv1", [128, 512], F32)]
    pw = [nc.alloc_psum_tensor("pw0", [128, 1024], F32),
          nc.alloc_psum_tensor("pw1", [128, 1024], F32)]
    fbank = [[pk[0], pk[1]], [pv[0], pv[1]]]
    fbn = [["pk0", "pk1"], ["pv0", "pv1"]]

    braw = rwg[:].bitcast(U8)[:, 0:4096]
    tebuf = rwg[:].bitcast(U8)[:, 4096:8192]
    mtb = xtmb[:].bitcast(U8)[:, 0:4096]
    mob = xtmb[:].bitcast(U8)[:, 4096:8192]

    bufs = {}
    for n in ("wkbf wvbf wrbf wobf w8bf ident ones_bf xbf0 xbf1 "
              "wk2 wk2b wv2 wv2b wr2 wr2b lnscr rwbt tm1k tm1v tm1r "
              "Ab Bb E1 Cb Db xtmb rwg zb x2b qbfb qfmb "
              "invc biasbf hselt tmkv tmvv tmrv tdv tfv wsv ew eu ruu "
              "w2 tia tib m0 m1 kmx ta4 negm dm4 rsc cgv chv cig cih s1v "
              "s1v4 s2v mea ex2 msq varv lnv istd ngm amx xsv i2x stv invs pk0 "
              "pk1 pv0 pv1 pw0 pw1 hreg rwgL rwgR hregA hregB "
              "hregC hregD").split():
        bufs[n] = Buf(n)
    for i in range(8):
        bufs[f"cc_in{i}"] = Buf(f"cc_in{i}")
        bufs[f"cc_out{i}"] = Buf(f"cc_out{i}")

    def vop(eng, emit, reads=(), writes=(), extra=(), key=None, inc=1):
        deps = list(extra)
        for bn in reads:
            deps.append(bufs[bn].w)
        for bn in writes:
            deps.extend(bufs[bn].r)
            deps.append(bufs[bn].w)
        t = s.op(eng, emit, deps, key=key, inc=inc)
        for bn in reads:
            bufs[bn].r.append(t)
        for bn in writes:
            bufs[bn].w = t
            bufs[bn].r = []
        return t

    vop("sync", lambda e: e.dma_start(tmkv[:], tmk_in[:]), writes=["tmkv"],
        key="vtmk", inc=16)
    vop("sync", lambda e: e.dma_start(tmvv[:], tmv_in[:]), writes=["tmvv"],
        key="vtmv", inc=16)
    vop("sync", lambda e: e.dma_start(tmrv[:], tmr_in[:]), writes=["tmrv"],
        key="vtmr", inc=16)
    vop("sync", lambda e: e.dma_start(tdv[:], td_in[:]), writes=["tdv"],
        key="vtd", inc=16)
    vop("sync", lambda e: e.dma_start(tfv[:], tf_in[:]), writes=["tfv"],
        key="vtf", inc=16)
    vop("sync", lambda e: e.dma_start(wsv[:], ws_in[:]), writes=["wsv"],
        key="vws", inc=16)
    vop("gpsimd", lambda e: e.dma_start(biasbf[:], bias_in[:]),
        writes=["biasbf"], key="vbi", inc=16)
    FW8 = 8 * SP
    vop("gpsimd", lambda e: e.dma_start(wkbf[:], wk_in[:]), writes=["wkbf"],
        key="wwk", inc=16)
    vop("gpsimd", lambda e: e.dma_start(
        bass.AP(xbf[0], 1, [[FW8, 128], [SP, 8], [1, TC]]),
        bass.AP(xT, 0, [[T, 128], [T * 128, 8], [1, TC]])),
        writes=["xbf0"], key="xb0", inc=16)
    vop("gpsimd", lambda e: e.dma_start(wvbf[:], wv_in[:]), writes=["wvbf"],
        key="wwv", inc=16)
    vop("gpsimd", lambda e: e.dma_start(wrbf[:], wr_in[:]), writes=["wrbf"],
        key="wwr", inc=16)
    vop("gpsimd", lambda e: e.dma_start(
        bass.AP(xbf[1], 0, [[FW8, 128], [SP, 8], [1, TC + 1]]),
        bass.AP(xT, TC - 1, [[T, 128], [T * 128, 8], [1, TC + 1]])),
        writes=["xbf1"], key="xb1", inc=16)
    vop("gpsimd", lambda e: e.dma_start(wobf[:], wo_in[:]), writes=["wobf"],
        key="wwo", inc=16)
    vop("scalar", lambda e: e.activation(tia[:], tdv[:], ACT.Exp),
        reads=["tdv"], writes=["tia"])
    vop("vector", lambda e: e.tensor_scalar(tib[:], tia[:], -1.0, None,
                                            op0=ALU.mult),
        reads=["tia"], writes=["tib"])
    vop("scalar", lambda e: e.activation(ew[:], tib[:], ACT.Exp),
        reads=["tib"], writes=["ew"])
    vop("scalar", lambda e: e.activation(eu[:], tfv[:], ACT.Exp),
        reads=["tfv"], writes=["eu"])
    vop("vector", lambda e: e.tensor_scalar(ruu[:], tfv[:], 0.0, None,
                                            op0=ALU.max),
        reads=["tfv"], writes=["ruu"])
    vop("vector", lambda e: e.tensor_scalar(w2[:], wsv[:], 2.0 / 448.0,
                                            None, op0=ALU.mult),
        reads=["wsv"], writes=["w2"])
    rg0 = nc.gpsimd.alloc_register("rg0")
    rg1 = nc.gpsimd.alloc_register("rg1")
    vop("sync", lambda e: e.dma_start(hselt[:], hs_in[:]), writes=["hselt"],
        key="vhs", inc=16)
    hld = vop("gpsimd", lambda e: e.reg_load(rg0, hselt[0:1, 0:1]),
              reads=["hselt"])
    hld = vop("gpsimd", lambda e: e.reg_mul(rg0, rg0, 1024 * TC), extra=[hld])
    hld = vop("gpsimd", lambda e: e.reg_add(rg1, rg0, 1024 * TC), extra=[hld])
    bufs["hreg"].w = hld
    bufs["hreg2"] = Buf("hreg2")
    rgA = nc.gpsimd.alloc_register("rgA")
    rgB = nc.gpsimd.alloc_register("rgB")
    rgC = nc.gpsimd.alloc_register("rgC")
    rgD = nc.gpsimd.alloc_register("rgD")
    vop("gpsimd", lambda e: e.reg_load(rgA, hselt[0:1, 1:2]),
        reads=["hselt"], writes=["hregA"])
    vop("gpsimd", lambda e: e.reg_load(rgC, hselt[0:1, 2:3]),
        reads=["hselt"], writes=["hregC"])
    vop("gpsimd", lambda e: e.memset(ones_bf[:], 1.0), writes=["ones_bf"])
    vop("gpsimd", lambda e: e.affine_select(
        ident[:], ones_bf[:], pattern=[[1, 128]], compare_op=ALU.is_equal,
        fill=0.0, base=0, channel_multiplier=-1),
        reads=["ones_bf"], writes=["ident"])
    vop("gpsimd", lambda e: e.memset(mpr[0][:], -1e38), writes=["m0"])
    vop("gpsimd", lambda e: e.memset(tib[:, 0:1], -0.6931471805599453),
        reads=["ew"], writes=["tib"])
    vop("gpsimd", lambda e: e.memset(tib[:, 1:2], 1e-5),
        reads=["ew"], writes=["tib"])
    vop("gpsimd", lambda e: e.memset(cgv[:], 0.0), writes=["cgv"])
    vop("gpsimd", lambda e: e.memset(chv[:], 0.0), writes=["chv"])
    for ct in range(8):
        vop("gpsimd",
            lambda e, ct=ct: e.memset(xbf[0][:, ct * SP:ct * SP + 1], 0.0),
            writes=["xbf0"])
    for t1, t1n, tmt, tmn in ((tm1k, "tm1k", tmkv, "tmkv"),
                              (tm1v, "tm1v", tmvv, "tmvv"),
                              (tm1r, "tm1r", tmrv, "tmrv")):
        vop("vector", lambda e, t1=t1, tmt=tmt: e.tensor_scalar(
            t1[:], tmt[:], -1.0, 1.0, op0=ALU.mult, op1=ALU.add),
            reads=[tmn], writes=[t1n])
    for wsrc, wsn, tmt, tmn, t1, t1n, wd, wdn, wdb, wdbn in (
            (wkbf, "wkbf", tmkv, "tmkv", tm1k, "tm1k",
             wk2, "wk2", wk2b, "wk2b"),
            (wvbf, "wvbf", tmvv, "tmvv", tm1v, "tm1v",
             wv2, "wv2", wv2b, "wv2b"),
            (wrbf, "wrbf", tmrv, "tmrv", tm1r, "tm1r",
             wr2, "wr2", wr2b, "wr2b")):
        for ct in range(8):
            sl = slice(ct * 512, ct * 512 + 512)
            vop("scalar", lambda e, sl=sl, ct=ct, wd=wd, wsrc=wsrc, tmt=tmt:
                e.activation(wd[:, sl], wsrc[:, sl], ACT.Identity,
                             bias=0.0, scale=tmt[:, ct:ct + 1]),
                reads=[wsn, tmn], writes=[wdn])
            vop("vector", lambda e, sl=sl, ct=ct, wdb=wdb, wsrc=wsrc, t1=t1:
                e.tensor_tensor(wdb[:, sl], wsrc[:, sl],
                                bcol(t1, ct, 512, freew=8), op=ALU.mult),
                reads=[wsn, t1n], writes=[wdbn])
    braw4 = zb[:].bitcast(U8)[:, 0:2048]
    tebuf4 = zb[:].bitcast(U8)[:, 4096:6144]
    mtb4 = x2b[:].bitcast(U8)[:, 0:2048]
    mob4 = x2b[:].bitcast(U8)[:, 4096:6144]

    def emit_decode(q):
        sl = slice(q * 2048, q * 2048 + 2048)
        vop("sync", lambda e, sl=sl: e.dma_start(braw4, w8_in[:, sl]),
            writes=["zb"], key="w8", inc=16)
        vop("vector", lambda e: e.tensor_scalar(tebuf4, braw4, 0x78, None,
                                                op0=ALU.bitwise_and),
            reads=["zb"], writes=["zb"])
        vop("vector", lambda e: e.tensor_scalar(mtb4, tebuf4, 0x78, None,
                                                op0=ALU.is_equal),
            reads=["zb"], writes=["x2b"])
        vop("vector", lambda e: e.tensor_scalar(mob4, tebuf4, 0x78, None,
                                                op0=ALU.is_lt),
            reads=["zb"], writes=["x2b"])
        vop("vector", lambda e: e.scalar_tensor_tensor(
            tebuf4, braw4, 8, mtb4, op0=ALU.subtract, op1=ALU.mult),
            reads=["zb", "x2b"], writes=["zb"])
        vop("vector", lambda e: e.tensor_tensor(mtb4, braw4, mob4,
                                                op=ALU.mult),
            reads=["zb", "x2b"], writes=["x2b"])
        vop("vector", lambda e, sl=sl: e.scalar_tensor_tensor(
            w8bf[:, sl], tebuf4.bitcast(FP8), 2.0, mtb4.bitcast(FP8),
            op0=ALU.mult, op1=ALU.add),
            reads=["zb", "x2b"], writes=["w8bf"])

    def emit_front(j):
        slot = j % 2
        xb = xbf[slot]
        xbn = f"xbf{slot}"
        FW = 8 * SP
        if j == 0:
            src = bass.AP(xT, 0, [[T, 128], [T * 128, 8], [1, TC]])
            dst = bass.AP(xb, 1, [[FW, 128], [SP, 8], [1, TC]])
        else:
            src = bass.AP(xT, j * TC - 1, [[T, 128], [T * 128, 8],
                                           [1, TC + 1]])
            dst = bass.AP(xb, 0, [[FW, 128], [SP, 8], [1, TC + 1]])
        vop("gpsimd", lambda e, srcx=src, dstx=dst: e.dma_start(dstx, srcx),
            writes=[xbn], key=f"xb{slot}", inc=16)

    def mm_group(pbuf, pbn, emits, reads):
        tok = None
        for q, em in enumerate(emits):
            tok = vop("tensor", em, reads=reads,
                      writes=[pbn] if q == 0 else ())
        bufs[pbn].w = tok
        return tok

    def mm_group2(pbuf, pbn, emits):
        tok = None
        for q, (em, rd) in enumerate(emits):
            tok = vop("tensor", em, reads=rd,
                      writes=[pbn] if q == 0 else ())
        bufs[pbn].w = tok
        return tok

    def proj_ems(pb, wd, wdb, xb, dt):
        ems = []
        for ct in range(8):
            ems.append(lambda e, ct=ct, pb=pb, wd=wd, xb=xb, dt=dt: e.matmul(
                pb[:], wd[:, ct * 512 + dt * 128:ct * 512 + dt * 128 + 128],
                xb[:, ct * SP + 1:ct * SP + 1 + TC],
                start=(ct == 0), stop=False))
            ems.append(lambda e, ct=ct, pb=pb, wdb=wdb, xb=xb, dt=dt:
                       e.matmul(
                pb[:], wdb[:, ct * 512 + dt * 128:ct * 512 + dt * 128 + 128],
                xb[:, ct * SP:ct * SP + TC],
                start=False, stop=(ct == 7)))
        return ems

    def emit_back_a(j, mid=None):
        slot = j % 2
        xb = xbf[slot]
        xbn = f"xbf{slot}"
        mp, mn = mpr[j % 2], mpr[(j + 1) % 2]
        mpn, mnn = f"m{j % 2}", f"m{(j + 1) % 2}"
        cstep = TC + 1
        for half in range(2):
            h2 = slice(2 * half, 2 * half + 2)
            for sub in range(2):
                dt = 2 * half + sub
                mm_group(pk[sub], f"pk{sub}",
                         proj_ems(pk[sub], wk2, wk2b, xb, dt),
                         ["wk2", "wk2b", xbn])
            for sub in range(2):
                dt = 2 * half + sub
                mm_group(pv[sub], f"pv{sub}",
                         proj_ems(pv[sub], wv2, wv2b, xb, dt),
                         ["wv2", "wv2b", xbn])
            for sub in range(2):
                dt = 2 * half + sub
                vop("vector", lambda e, dt=dt, sub=sub: e.reduce_max(
                    kmx[:, dt:dt + 1], pk[sub][:], axis=AX),
                    reads=[f"pk{sub}"], writes=["kmx"])
            vop("vector", lambda e, h2=h2: e.tensor_add(
                ta4[:, h2], kmx[:, h2], ruu[:, h2]),
                reads=["kmx", "ruu"], writes=["ta4"])
            vop("vector", lambda e, h2=h2, mp=mp, mn=mn: e.tensor_max(
                mn[:, h2], mp[:, h2], ta4[:, h2]),
                reads=[mpn, "ta4"], writes=[mnn])
            vop("vector", lambda e, h2=h2, mn=mn: e.tensor_scalar(
                negm[:, h2], mn[:, h2], -1.0, None, op0=ALU.mult),
                reads=[mnn], writes=["negm"])
            vop("vector", lambda e, h2=h2, mp=mp, mn=mn: e.tensor_sub(
                dm4[:, h2], mp[:, h2], mn[:, h2]),
                reads=[mpn, mnn], writes=["dm4"])
            vop("scalar", lambda e, h2=h2: e.activation(
                rsc[:, h2], dm4[:, h2], ACT.Exp),
                reads=["dm4"], writes=["rsc"])
            vop("vector", lambda e, h2=h2: e.tensor_mul(
                cig[:, h2], cgv[:, h2], rsc[:, h2]),
                reads=["cgv", "rsc"], writes=["cig"])
            vop("vector", lambda e, h2=h2: e.tensor_mul(
                cih[:, h2], chv[:, h2], rsc[:, h2]),
                reads=["chv", "rsc"], writes=["cih"])
            for sub in range(2):
                dt = 2 * half + sub
                vop("scalar", lambda e, dt=dt, sub=sub: e.activation(
                    Ab[:, dt * 512:(dt + 1) * 512], pk[sub][:], ACT.Exp,
                    bias=negm[:, dt:dt + 1], scale=1.0),
                    reads=[f"pk{sub}", "negm"], writes=["Ab"])
                vop("vector", lambda e, dt=dt, sub=sub: e.tensor_mul(
                    Bb[:, dt * 512:(dt + 1) * 512],
                    Ab[:, dt * 512:(dt + 1) * 512], pv[sub][:]),
                    reads=["Ab", f"pv{sub}"], writes=["Bb"])
        rbank = [pk[0], pk[1], pv[0], pv[1]]
        rbn = ["pk0", "pk1", "pv0", "pv1"]
        for dt in range(4):
            mm_group(rbank[dt], rbn[dt],
                     proj_ems(rbank[dt], wr2, wr2b, xb, dt),
                     ["wr2", "wr2b", xbn])
            vop("scalar", lambda e, dt=dt: e.activation(
                rwbt[:, dt * 512:(dt + 1) * 512], rbank[dt][:], ACT.Tanh,
                bias=0.0, scale=0.5),
                reads=[rbn[dt]], writes=["rwbt"])
        if mid is not None:
            mid()
        vop("vector", lambda e: e.tensor_copy(
            bass.AP(Cb, 0, [[4 * cstep, 128], [cstep, 4], [1, 1]]),
            bass.AP(cig, 0, [[4, 128], [1, 4], [1, 1]])),
            reads=["cig"], writes=["Cb"])
        vop("vector", lambda e: e.tensor_copy(
            bass.AP(Db, 0, [[4 * cstep, 128], [cstep, 4], [1, 1]]),
            bass.AP(cih, 0, [[4, 128], [1, 4], [1, 1]])),
            reads=["cih"], writes=["Db"])
        for dt in range(4):
            base = dt * cstep
            vop("vector", lambda e, dt=dt, base=base: e.tensor_tensor_scan(
                Cb[:, base + 1:base + 1 + TC], bcol(ew, dt, TC),
                Bb[:, dt * 512:(dt + 1) * 512], cig[:, dt:dt + 1],
                op0=ALU.mult, op1=ALU.add),
                reads=["ew", "Bb", "cig"], writes=["Cb"])
            vop("vector", lambda e, dt=dt, base=base: e.tensor_tensor_scan(
                Db[:, base + 1:base + 1 + TC], bcol(ew, dt, TC),
                Ab[:, dt * 512:(dt + 1) * 512], cih[:, dt:dt + 1],
                op0=ALU.mult, op1=ALU.add),
                reads=["ew", "Ab", "cih"], writes=["Db"])
        for dt in range(4):
            base = dt * cstep
            vop("vector", lambda e, dt=dt, base=base: e.scalar_tensor_tensor(
                E1[:, dt * 512:(dt + 1) * 512],
                Bb[:, dt * 512:(dt + 1) * 512], eu[:, dt:dt + 1],
                Cb[:, base:base + TC], op0=ALU.mult, op1=ALU.add),
                reads=["Bb", "eu", "Cb"], writes=["E1"])
        vop("vector", lambda e: e.tensor_copy(
            bass.AP(cgv, 0, [[4, 128], [1, 4], [1, 1]]),
            bass.AP(Cb, TC, [[4 * cstep, 128], [cstep, 4], [1, 1]])),
            reads=["Cb"], writes=["cgv"])
        vop("vector", lambda e: e.tensor_copy(
            bass.AP(chv, 0, [[4, 128], [1, 4], [1, 1]]),
            bass.AP(Db, TC, [[4 * cstep, 128], [cstep, 4], [1, 1]])),
            reads=["Db"], writes=["chv"])
        for dt in range(4):
            base = dt * cstep
            vop("vector", lambda e, dt=dt, base=base:
                e.scalar_tensor_tensor(
                    Cb[:, base:base + TC],
                    Ab[:, dt * 512:(dt + 1) * 512], eu[:, dt:dt + 1],
                    Db[:, base:base + TC], op0=ALU.mult, op1=ALU.add),
                reads=["Ab", "eu", "Db", "cgv"], writes=["Cb"])
        for dt in range(4):
            base = dt * cstep
            vop("scalar", lambda e, base=base: e.activation(
                lnscr[:], Cb[:, base:base + TC], ACT.Ln),
                reads=["Cb"], writes=["lnscr"])
            vop("scalar", lambda e, dt=dt: e.activation(
                Bb[:, dt * 512:(dt + 1) * 512], lnscr[:], ACT.Exp,
                bias=tib[:, 0:1], scale=-1.0),
                reads=["lnscr", "tib"], writes=["Bb"])

    def emit_back_b(j):
        cstep = TC + 1
        wkv3 = bass.AP(Cb, 0, [[4 * cstep, 128], [cstep, 4], [1, TC]])
        e13 = bass.AP(E1, 0, [[2048, 128], [512, 4], [1, TC]])
        bb3 = bass.AP(Bb, 0, [[2048, 128], [512, 4], [1, TC]])
        vop("vector", lambda e, o=wkv3, a=e13, bq=bb3: e.tensor_tensor(
            o, a, bq, op=ALU.mult),
            reads=["E1", "Bb"], writes=["Cb"])
        vop("vector", lambda e, w=wkv3: e.scalar_tensor_tensor(
            srb, rwbt[:], 1.0, w, op0=ALU.add, op1=ALU.mult),
            reads=["rwbt", "Cb"], writes=["Db"])
        src2 = bass.AP(Db.bitcast(BF16), 2048, [[4104, 128], [512, 4],
                                                [1, 512]])
        dst2 = bass.AP(cc_big, j * 512 * TC,
                       [[TC, 128], [128 * TC, 4], [1, TC]])
        vop("sync", lambda e, a=dst2, bq=src2: e.dma_start(a, bq),
            reads=["Db"], writes=[f"cc_in{j}"], key=f"cci{j}", inc=16)

    groups = [[0, 1], [2, 3], [4, 5], [6, 7]]

    def emit_cc(j):
        in_ap = bass.AP(cc_big, j * 512 * TC, [[TC, 512], [1, TC]])
        out_ap = bass.AP(cc_out_all, j * 1024 * TC, [[TC, 1024], [1, TC]])
        vop("gpsimd", lambda e, in_ap=in_ap, out_ap=out_ap:
            e.collective_compute(
                "AllGather", ALU.bypass, replica_groups=groups,
                ins=[in_ap.opt()], outs=[out_ap.opt()]),
            reads=[f"cc_in{j}"], writes=[f"cc_out{j}"], key=f"cc{j}", inc=1)

    def emit_phase2(i):
        def sub_a(hh):
            if hh == 0:
                vop("gpsimd", lambda e, base=2 * i * 512 * TC:
                    e.reg_add(rgB, rgA, base),
                    reads=["hregA"], writes=["hregB"])
                dstL = bass.AP(rwg, 0, [[4096, 128], [512, 4], [1, TC]])
                vop("gpsimd", lambda e, a=dstL: e.dma_start(
                    a, bass.AP(cc_big, rgB,
                               [[TC, 128], [128 * TC, 4], [1, TC]])),
                    reads=["hregB"], writes=["rwgL"],
                    extra=[bufs[f"cc_in{2 * i}"].w,
                           bufs[f"cc_in{2 * i + 1}"].w],
                    key="rwgL", inc=16)
                vop("gpsimd", lambda e, base=2 * i * 1024 * TC:
                    e.reg_add(rgD, rgC, base),
                    reads=["hregC"], writes=["hregD"])
                dstR = bass.AP(rwg, 2048, [[4096, 128], [512, 4], [1, TC]])
                vop("gpsimd", lambda e, a=dstR: e.dma_start(
                    a, bass.AP(cc_out_all, rgD,
                               [[TC, 128], [128 * TC, 4], [1, TC]])),
                    reads=["hregD"], writes=["rwgR"],
                    extra=[bufs[f"cc_out{2 * i}"].w,
                           bufs[f"cc_out{2 * i + 1}"].w],
                    key="rwgR", inc=16)
            row0 = 512 * i + 256 * hh
            srct = bass.AP(x_tm, row0 * C, [[C, 128], [128 * C, 2], [1, C]])
            dstt = bass.AP(xtmb, 0, [[2 * C, 128], [C, 2], [1, C]])
            vop("sync", lambda e, a=dstt, bq=srct: e.dma_start(a, bq),
                writes=["xtmb"], key="xtm", inc=16)
            for tt in range(2):
                tbase = hh * 256 + tt * 128
                ems = []
                for dt in list(range(4)) + list(range(4, 8)):
                    for nn in range(2):
                        ems.append((lambda e, nn=nn, dt=dt, pb=pw[tt],
                                    tbase=tbase: e.matmul(
                            pb[:, nn * 512:(nn + 1) * 512],
                            rwg[:, dt * 512 + tbase:dt * 512 + tbase + 128],
                            wobf[:, dt * 1024 + nn * 512:
                                 dt * 1024 + (nn + 1) * 512],
                            start=(dt == 0), stop=(dt == 7)),
                            ["rwgL" if dt < 4 else "rwgR", "wobf"]))
                mm_group2(pw[tt], f"pw{tt}", ems)
            return []

        def sub_b(hh):
            row0 = 512 * i + 256 * hh
            for tt in range(2):
                vop("vector", lambda e, tt=tt: e.scalar_tensor_tensor(
                    zb[:, tt * C:(tt + 1) * C], pw[tt][:], 1.0,
                    xtmb[:, tt * C:(tt + 1) * C], op0=ALU.mult, op1=ALU.add,
                    accum_out=s1v[:, tt:tt + 1]),
                    reads=[f"pw{tt}", "xtmb"], writes=["zb", "s1v"])
            for tt in range(2):
                vop("scalar", lambda e, tt=tt: e.activation(
                    qbfb[:].bitcast(F32)[:, 0:C],
                    zb[:, tt * C:(tt + 1) * C], ACT.Square,
                    accum_out=s2v[:, tt:tt + 1]),
                    reads=["zb"], writes=["qbfb", "s2v"])
            ln_stats()
            for tt in range(2):
                vop("scalar", lambda e, tt=tt: e.activation(
                    x2b[:, tt * C:(tt + 1) * C], zb[:, tt * C:(tt + 1) * C],
                    ACT.Identity, bias=ngm[:, tt:tt + 1],
                    scale=istd[:, tt:tt + 1]),
                    reads=["zb", "ngm", "istd"], writes=["x2b"])
            for tt in range(2):
                vop("vector", lambda e, tt=tt: e.reduce_max(
                    amx[:, tt:tt + 1], x2b[:, tt * C:(tt + 1) * C], axis=AX,
                    apply_absolute_value=True),
                    reads=["x2b"], writes=["amx"])
            vop("vector", lambda e: e.tensor_scalar(
                stv[:], amx[:], 1e-12, w2[:, 0:1], op0=ALU.max,
                op1=ALU.mult),
                reads=["amx", "w2"], writes=["stv"])
            vop("vector", lambda e: e.reciprocal(invs[:], stv[:]),
                reads=["stv"], writes=["invs"])
            vop("vector", lambda e: e.tensor_scalar(
                i2x[:], invs[:], wsv[:, 0:1], None, op0=ALU.mult),
                reads=["invs", "wsv"], writes=["i2x"])
            vop("scalar", lambda e: e.activation(invc[:], invs[:], ACT.Copy),
                reads=["invs"], writes=["invc"])
            for tt in range(2):
                vop("scalar", lambda e, tt=tt: e.activation(
                    q8b[:, tt * C:(tt + 1) * C], x2b[:, tt * C:(tt + 1) * C],
                    ACT.Copy, bias=0.0, scale=i2x[:, tt:tt + 1]),
                    reads=["x2b", "i2x"], writes=["qfmb"])
            vop("scalar", lambda e: e.activation(qbfb[:], q8b, ACT.Copy),
                reads=["qfmb"], writes=["qbfb"])
            return []

        def sub_c(hh):
            for tt in range(2):
                pb, pbn = pw[tt], f"pw{tt}"
                tok = None
                for bqi in range(8):
                    tok = vop("tensor", lambda e, tt=tt, bqi=bqi, pb=pb:
                              e.transpose(
                                  pb[:].bitcast(BF16)[:, bqi * 128:
                                                      (bqi + 1) * 128],
                                  qbfb[:, tt * C + bqi * 128:
                                       tt * C + (bqi + 1) * 128],
                                  ident[:]),
                              reads=["qbfb", "ident"],
                              writes=[pbn] if bqi == 0 else ())
                bufs[pbn].w = tok
                tok = vop("tensor", lambda e, tt=tt, pb=pb: e.transpose(
                    pb[:].bitcast(BF16)[0:1, 1024:1152],
                    invc[:, tt:tt + 1], ident[:]),
                    reads=["invc", "ident"])
                bufs[pbn].w = tok
                vop("scalar", lambda e, tt=tt, pb=pb: e.activation(
                    bass.AP(qfmb, tt * 128, [[2048, 128], [256, 8], [1, 128]]),
                    bass.AP(pb.bitcast(BF16), 0,
                            [[2048, 128], [128, 8], [1, 128]]),
                    ACT.Copy),
                    reads=[pbn], writes=["qfmb"])
                vop("scalar", lambda e, tt=tt, pb=pb: e.activation(
                    qinv[0:1, tt * 128:(tt + 1) * 128],
                    pb[:].bitcast(BF16)[0:1, 1024:1152], ACT.Copy),
                    reads=[pbn], writes=["qbfb"])
            for tt in range(2):
                for nn in range(2):
                    pb = fbank[tt][nn]
                    ems = []
                    for kt in range(8):
                        ems.append((lambda e, nn=nn, kt=kt, tt=tt, pb=pb:
                                    e.matmul(
                            pb[:],
                            qfmb[:, kt * 256 + tt * 128:
                                 kt * 256 + tt * 128 + 128],
                            w8bf[:, kt * 1024 + nn * 512:
                                 kt * 1024 + (nn + 1) * 512],
                            start=(kt == 0), stop=False),
                            ["qfmb", "w8bf"]))
                    ems.append((lambda e, nn=nn, tt=tt, pb=pb: e.matmul(
                        pb[:],
                        qinv[0:1, tt * 128:(tt + 1) * 128],
                        biasbf[0:1, nn * 512:(nn + 1) * 512],
                        start=False, stop=True),
                        ["qbfb", "biasbf"]))
                    mm_group2(pb, fbn[tt][nn], ems)
            return []

        def sub_d(hh):
            row0 = 512 * i + 256 * hh
            for tt in range(2):
                for nn in range(2):
                    sl = slice(tt * C + nn * 512, tt * C + nn * 512 + 512)
                    vop("vector", lambda e, tt=tt, nn=nn, sl=sl:
                        e.scalar_tensor_tensor(
                            zb[:, sl], fbank[tt][nn][:],
                            stv[:, tt:tt + 1], x2b[:, sl],
                            op0=ALU.mult, op1=ALU.add,
                            accum_out=s1v4[:, 2 * tt + nn:2 * tt + nn + 1]),
                        reads=[fbn[tt][nn], "stv", "x2b"],
                        writes=["zb", "s1v4"])
            vop("vector", lambda e: e.tensor_tensor(
                s1v[:], bass.AP(s1v4, 0, [[4, 128], [2, 2]]),
                bass.AP(s1v4, 1, [[4, 128], [2, 2]]), op=ALU.add),
                reads=["s1v4"], writes=["s1v"])
            for tt in range(2):
                vop("scalar", lambda e, tt=tt: e.activation(
                    qbfb[:].bitcast(F32)[:, 0:C],
                    zb[:, tt * C:(tt + 1) * C], ACT.Square,
                    accum_out=s2v[:, tt:tt + 1]),
                    reads=["zb"], writes=["qbfb", "s2v"])
            ln_stats()
            for tt in range(2):
                vop("scalar", lambda e, tt=tt: e.activation(
                    x2b[:, tt * C:(tt + 1) * C], zb[:, tt * C:(tt + 1) * C],
                    ACT.Identity, bias=ngm[:, tt:tt + 1],
                    scale=istd[:, tt:tt + 1]),
                    reads=["zb", "ngm", "istd"], writes=["x2b"])
            srcy = bass.AP(x2b, 0, [[2 * C, 128], [C, 2], [1, C]])
            dsty = bass.AP(y_out, row0 * C, [[C, 128], [128 * C, 2], [1, C]])
            return [vop("sync", lambda e, a=dsty, bq=srcy: e.dma_start(a, bq),
                        reads=["x2b"], key="yout", inc=16)]

        def ln_stats():
            vop("vector", lambda e: e.tensor_scalar(
                mea[:], s1v[:], 1.0 / C, None, op0=ALU.mult),
                reads=["s1v"], writes=["mea"])
            vop("vector", lambda e: e.tensor_mul(msq[:], mea[:], mea[:]),
                reads=["mea"], writes=["msq"])
            vop("vector", lambda e: e.scalar_tensor_tensor(
                lnv[:], s2v[:], 1.0 / C, msq[:], op0=ALU.mult,
                op1=ALU.subtract),
                reads=["s2v", "msq"], writes=["lnv"])
            vop("scalar", lambda e: e.activation(
                varv[:], lnv[:], ACT.Ln, bias=tib[:, 1:2], scale=1.0),
                reads=["lnv", "tib"], writes=["varv"])
            vop("scalar", lambda e: e.activation(
                istd[:], varv[:], ACT.Exp, bias=0.0, scale=-0.5),
                reads=["varv"], writes=["istd"])
            vop("vector", lambda e: e.scalar_tensor_tensor(
                ngm[:], mea[:], -1.0, istd[:], op0=ALU.mult, op1=ALU.mult),
                reads=["mea", "istd"], writes=["ngm"])

        ab, cd = [], []
        for hh in range(2):
            ab.append((lambda hh=hh: sub_a(hh), lambda hh=hh: sub_b(hh)))
            cd.append((lambda hh=hh: sub_c(hh), lambda hh=hh: sub_d(hh)))
        return ab, cd

    out_toks = []
    abq, cdq = [], []
    ab_count = [0]
    cd_count = [0]
    pend_b = []
    pend_d = []

    def pop_c(j):
        if cdq and j >= cd_count[0] + 3:
            c, d = cdq.pop(0)
            c()
            pend_d.append(d)
            cd_count[0] += 1

    def pop_d():
        while pend_d:
            out_toks.extend(pend_d.pop(0)())

    def pop_a(j):
        if abq and j >= ab_count[0] + 2:
            a, b = abq.pop(0)
            a()
            pend_b.append(b)
            ab_count[0] += 1

    def pop_b():
        while pend_b:
            pend_b.pop(0)()

    for j in range(NCH):
        if 2 <= j + 1 < NCH:
            emit_front(j + 1)
        if j % 2 == 1:
            ab, cd = emit_phase2(j // 2)
            abq.extend(ab)
            cdq.extend(cd)
        emit_back_a(j, mid=lambda jj=j: (pop_c(jj), pop_a(jj)))
        if j < 2:
            emit_decode(j)
        if j == 2:
            emit_decode(2)
            emit_decode(3)
        pop_d()
        emit_back_b(j)
        emit_cc(j)
        pop_b()
    j = NCH
    while abq or cdq or pend_b or pend_d:
        pop_c(j)
        pop_a(j)
        pop_d()
        pop_b()
        j += 1

    s.op("gpsimd", lambda e: e.engine_nop(), deps=out_toks)

    s.emit()
    nc.finalize()
    return nc


_NC_CACHE = [None]


def kernel(L0_attn_x, L0_attn_time_decay, L0_attn_time_first,
           L0_attn_time_mix_k, L0_attn_time_mix_v, L0_attn_time_mix_r,
           L0_attn_Wk, L0_attn_Wv, L0_attn_Wr, L0_attn_Wo,
           L0_ffn_weight, L0_ffn_weight_scale, L0_ffn_bias):
    x = np.ascontiguousarray(np.asarray(L0_attn_x, dtype=np.float32))

    def vec128(v):
        v = np.ascontiguousarray(np.asarray(v, dtype=np.float32))
        return np.ascontiguousarray(v.reshape(-1, 128).T)

    def wlay(wt, cols):
        r = wt.shape[0]
        return np.ascontiguousarray(
            wt.reshape(r // 128, 128, cols).transpose(1, 0, 2)
            .reshape(128, (r // 128) * cols))

    wkT = np.asarray(L0_attn_Wk, dtype=np.float32).T
    wvT = np.asarray(L0_attn_Wv, dtype=np.float32).T
    wrT = np.asarray(L0_attn_Wr, dtype=np.float32).T
    woT = np.asarray(L0_attn_Wo, dtype=np.float32).T
    w8T = np.ascontiguousarray(
        np.asarray(L0_ffn_weight).view(np.uint8).T)
    ws = np.float32(np.asarray(L0_ffn_weight_scale, dtype=np.float32))
    bias = np.ascontiguousarray(
        np.asarray(L0_ffn_bias, dtype=np.float32)[None, :])

    tmk = vec128(L0_attn_time_mix_k)
    tmv = vec128(L0_attn_time_mix_v)
    tmr = vec128(L0_attn_time_mix_r)
    td = np.asarray(L0_attn_time_decay, dtype=np.float32)
    tf = np.asarray(L0_attn_time_first, dtype=np.float32)
    wo_l = wlay(np.ascontiguousarray(woT), 1024)
    wo8 = wo_l.reshape(128, 8, 1024)
    wo_swap = np.ascontiguousarray(
        np.concatenate([wo8[:, 4:], wo8[:, :4]], axis=1).reshape(128, 8192))
    w8_l = wlay(w8T, 1024)

    if _NC_CACHE[0] is None:
        _NC_CACHE[0] = build_nc()
    nc = _NC_CACHE[0]

    in_maps = []
    for core in range(8):
        b, h = core // 2, core % 2
        H = slice(512 * h, 512 * h + 512)
        xTb = np.ascontiguousarray(x[b].T)
        rows = [x[b, 512 * j:512 * j + 512] for j in range(h, 8, 2)]
        xtm = np.ascontiguousarray(np.concatenate(rows, axis=0))
        in_maps.append({
            "xT": xTb,
            "x_tm": xtm,
            "wk_in": wlay(np.ascontiguousarray(wkT[:, H]), 512),
            "wv_in": wlay(np.ascontiguousarray(wvT[:, H]), 512),
            "wr_in": wlay(np.ascontiguousarray(wrT[:, H]), 512),
            "wo_in": wo_l if h == 0 else wo_swap,
            "w8_in": w8_l,
            "tmk_in": tmk, "tmv_in": tmv, "tmr_in": tmr,
            "td_in": vec128(td[H]), "tf_in": vec128(tf[H]),
            "ws_in": np.ascontiguousarray(
                np.broadcast_to(ws, (128, 1)).astype(np.float32)),
            "bias_in": bias,
            "hs_in": np.array(
                [[h, h * 512 * TC,
                  h * 1024 * TC + (1 - h) * 512 * TC]], dtype=np.int32),
        })

    res = run_bass_kernel_spmd(nc, in_maps, core_ids=list(range(8)))
    out = np.empty((B, T, C), dtype=np.float32)
    for core in range(8):
        b, h = core // 2, core % 2
        yo = np.asarray(res.results[core]["y_out"])
        for idx, j in enumerate(range(h, 8, 2)):
            out[b, 512 * j:512 * j + 512] = yo[512 * idx:512 * idx + 512]
    return out


# version 32 (final)
# speedup vs baseline: 1.1476x; 1.1476x over previous
import numpy as np
import concourse.bass as bass
import concourse.mybir as mybir
from concourse.bass_utils import run_bass_kernel_spmd

F32 = mybir.dt.float32
BF16 = mybir.dt.bfloat16
U8 = mybir.dt.uint8
FP8 = mybir.dt.float8e4
AX = mybir.AxisListType.X
ALU = mybir.AluOpType
ACT = mybir.ActivationFunctionType

ENGS = ("sync", "scalar", "vector", "tensor", "gpsimd")

B, T, C = 4, 4096, 1024
TC = 512
NCH = T // TC
SP = TC + 2


class Sched:

    def __init__(self, nc):
        self.nc = nc
        self.ops = {e: [] for e in ENGS}
        self.count = {}
        self.sems = {}
        self.dep_targets = set()

    def _chan(self, chan):
        if chan not in self.sems:
            self.sems[chan] = self.nc.alloc_semaphore(
                f"sem_{chan[0]}_{chan[1]}")
            self.count[chan] = 0
        return chan

    def op(self, eng, emit, deps=(), key=None, inc=1):
        chan = self._chan((eng, "cmp") if key is None else ("dma", key))
        deps = [d for d in deps if d is not None]
        for d in deps:
            self.dep_targets.add(d)
        self.count[chan] += inc
        tok = (chan, self.count[chan])
        self.ops[eng].append((emit, deps, tok))
        return tok

    def emit(self):
        with self.nc.Block() as block:
            for eng_name in ENGS:
                if not self.ops[eng_name]:
                    continue

                def body(engobj, eng_name=eng_name):
                    waited = {}
                    last_inc = {}
                    for emit_fn, deps, tok in self.ops[eng_name]:
                        byc = {}
                        for dchan, dv in deps:
                            byc[dchan] = max(byc.get(dchan, 0), dv)
                        for dchan, dv in sorted(byc.items(), key=str):
                            if dv > waited.get(dchan, 0):
                                engobj.wait_ge(self.sems[dchan], dv)
                                waited[dchan] = dv
                        inst = emit_fn(engobj)
                        chan, cum = tok
                        if tok in self.dep_targets:
                            inst.then_inc(self.sems[chan],
                                          cum - last_inc.get(chan, 0))
                            last_inc[chan] = cum

                getattr(block, eng_name)(body)


class Buf:

    def __init__(self, name):
        self.name = name
        self.w = None
        self.r = []


def bcol(t, col, n, freew=4):
    return bass.AP(t, col, [[freew, 128], [0, n]])


def build_nc():
    nc = bass.Bass("TRN2", target_bir_lowering=False)
    s = Sched(nc)

    xT = nc.dram_tensor("xT", [C, T], F32, kind="ExternalInput")
    x_tm = nc.dram_tensor("x_tm", [T // 2, C], F32, kind="ExternalInput")
    wk_in = nc.dram_tensor("wk_in", [128, 4096], F32, kind="ExternalInput")
    wv_in = nc.dram_tensor("wv_in", [128, 4096], F32, kind="ExternalInput")
    wr_in = nc.dram_tensor("wr_in", [128, 4096], F32, kind="ExternalInput")
    wo_in = nc.dram_tensor("wo_in", [128, 8192], F32, kind="ExternalInput")
    w8_in = nc.dram_tensor("w8_in", [128, 8192], U8, kind="ExternalInput")
    tmk_in = nc.dram_tensor("tmk_in", [128, 8], F32, kind="ExternalInput")
    tmv_in = nc.dram_tensor("tmv_in", [128, 8], F32, kind="ExternalInput")
    tmr_in = nc.dram_tensor("tmr_in", [128, 8], F32, kind="ExternalInput")
    td_in = nc.dram_tensor("td_in", [128, 4], F32, kind="ExternalInput")
    tf_in = nc.dram_tensor("tf_in", [128, 4], F32, kind="ExternalInput")
    ws_in = nc.dram_tensor("ws_in", [128, 1], F32, kind="ExternalInput")
    bias_in = nc.dram_tensor("bias_in", [1, C], F32, kind="ExternalInput")
    y_out = nc.dram_tensor("y_out", [T // 2, C], F32, kind="ExternalOutput")
    hs_in = nc.dram_tensor("hs_in", [1, 3], mybir.dt.int32,
                           kind="ExternalInput")
    cc_big = nc.dram_tensor("cc_big", [8 * 512, TC], BF16)
    cc_out_all = nc.dram_tensor("cc_out_all", [8 * 1024, TC], BF16)

    A = nc.alloc_sbuf_tensor
    wkbf = A("wkbf", [128, 4096], BF16)
    wvbf = A("wvbf", [128, 4096], BF16)
    wrbf = A("wrbf", [128, 4096], BF16)
    wobf = A("wobf", [128, 8192], BF16)
    w8bf = A("w8bf", [128, 8192], BF16)
    ident = A("ident", [128, 128], BF16)
    ones_bf = A("ones_bf", [128, 128], BF16)
    xbf = [A("xbf0", [128, 8 * SP], BF16), A("xbf1", [128, 8 * SP], BF16)]
    wk2 = A("wk2", [128, 4096], BF16)
    wk2b = A("wk2b", [128, 4096], BF16)
    wv2 = A("wv2", [128, 4096], BF16)
    wv2b = A("wv2b", [128, 4096], BF16)
    wr2 = A("wr2", [128, 4096], BF16)
    wr2b = A("wr2b", [128, 4096], BF16)
    lnscr = A("lnscr", [128, 512], F32)
    rwbt = A("rwbt", [128, 2048], BF16)
    Ab = A("Ab", [128, 2048], BF16)
    Bb = A("Bb", [128, 2048], F32)
    E1 = A("E1", [128, 2048], F32)
    Cb = A("Cb", [128, 4 * (TC + 1)], F32)
    Db = A("Db", [128, 4 * (TC + 1)], F32)
    xtmb = A("xtmb", [128, 2 * C], F32)
    rwg = A("rwg", [128, 4096], BF16)
    zb = A("zb", [128, 2048], F32)
    x2b = A("x2b", [128, 2048], F32)
    qbfb = A("qbfb", [128, 2048], BF16)
    qfmb = A("qfmb", [128, 2048], BF16)
    invc = A("invc", [128, 2], BF16)
    biasbf = A("biasbf", [1, C], BF16)
    hselt = A("hselt", [1, 3], mybir.dt.int32)
    tmkv = A("tmkv", [128, 8], F32)
    tmvv = A("tmvv", [128, 8], F32)
    tmrv = A("tmrv", [128, 8], F32)
    tdv = A("tdv", [128, 4], F32)
    tfv = A("tfv", [128, 4], F32)
    wsv = A("wsv", [128, 1], F32)
    ew = A("ew", [128, 4], F32)
    eu = A("eu", [128, 4], F32)
    ruu = A("ruu", [128, 4], F32)
    w2 = A("w2", [128, 1], F32)
    tia = A("tia", [128, 4], F32)
    tib = A("tib", [128, 4], F32)
    mpr = [A("m0", [128, 4], F32), A("m1", [128, 4], F32)]
    kmx = A("kmx", [128, 4], F32)
    ta4 = A("ta4", [128, 4], F32)
    negm = A("negm", [128, 4], F32)
    dm4 = A("dm4", [128, 4], F32)
    rsc = A("rsc", [128, 4], F32)
    cgv = A("cgv", [128, 4], F32)
    chv = A("chv", [128, 4], F32)
    cig = A("cig", [128, 4], F32)
    cih = A("cih", [128, 4], F32)
    s1v = A("s1v", [128, 2], F32)
    s1v4 = A("s1v4", [128, 4], F32)
    s2v = A("s2v", [128, 2], F32)
    mea = A("mea", [128, 2], F32)
    ex2 = A("ex2", [128, 2], F32)
    msq = A("msq", [128, 2], F32)
    varv = A("varv", [128, 2], F32)
    lnv = A("lnv", [128, 2], F32)
    istd = A("istd", [128, 2], F32)
    ngm = A("ngm", [128, 2], F32)
    amx = A("amx", [128, 2], F32)
    xsv = A("xsv", [128, 2], F32)
    i2x = A("i2x", [128, 2], F32)
    stv = A("stv", [128, 2], F32)
    invs = A("invs", [128, 2], F32)
    tm1k = A("tm1k", [128, 8], F32)
    tm1v = A("tm1v", [128, 8], F32)
    tm1r = A("tm1r", [128, 8], F32)

    srb = Db[:].bitcast(BF16)[:, 2048:4096]
    q8b = qfmb[:].bitcast(FP8)[:, 0:2048]
    qinv = qbfb
    pk = [nc.alloc_psum_tensor("pk0", [128, 512], F32),
          nc.alloc_psum_tensor("pk1", [128, 512], F32)]
    pv = [nc.alloc_psum_tensor("pv0", [128, 512], F32),
          nc.alloc_psum_tensor("pv1", [128, 512], F32)]
    pw = [nc.alloc_psum_tensor("pw0", [128, 1024], F32),
          nc.alloc_psum_tensor("pw1", [128, 1024], F32)]
    fbank = [[pk[0], pk[1]], [pv[0], pv[1]]]
    fbn = [["pk0", "pk1"], ["pv0", "pv1"]]

    braw = rwg[:].bitcast(U8)[:, 0:4096]
    tebuf = rwg[:].bitcast(U8)[:, 4096:8192]
    mtb = xtmb[:].bitcast(U8)[:, 0:4096]
    mob = xtmb[:].bitcast(U8)[:, 4096:8192]

    bufs = {}
    for n in ("wkbf wvbf wrbf wobf w8bf ident ones_bf xbf0 xbf1 "
              "wk2 wk2b wv2 wv2b wr2 wr2b lnscr rwbt tm1k tm1v tm1r "
              "Ab Bb E1 Cb Db xtmb rwg zb x2b qbfb qfmb "
              "invc biasbf hselt tmkv tmvv tmrv tdv tfv wsv ew eu ruu "
              "w2 tia tib m0 m1 kmx ta4 negm dm4 rsc cgv chv cig cih s1v "
              "s1v4 s2v mea ex2 msq varv lnv istd ngm amx xsv i2x stv invs pk0 "
              "pk1 pv0 pv1 pw0 pw1 hreg rwgL rwgR hregA hregB "
              "hregC hregD").split():
        bufs[n] = Buf(n)
    for i in range(8):
        bufs[f"cc_in{i}"] = Buf(f"cc_in{i}")
        bufs[f"cc_out{i}"] = Buf(f"cc_out{i}")

    def vop(eng, emit, reads=(), writes=(), extra=(), key=None, inc=1):
        deps = list(extra)
        for bn in reads:
            deps.append(bufs[bn].w)
        for bn in writes:
            deps.extend(bufs[bn].r)
            deps.append(bufs[bn].w)
        t = s.op(eng, emit, deps, key=key, inc=inc)
        for bn in reads:
            bufs[bn].r.append(t)
        for bn in writes:
            bufs[bn].w = t
            bufs[bn].r = []
        return t

    vop("sync", lambda e: e.dma_start(tmkv[:], tmk_in[:]), writes=["tmkv"],
        key="vtmk", inc=16)
    vop("sync", lambda e: e.dma_start(tmvv[:], tmv_in[:]), writes=["tmvv"],
        key="vtmv", inc=16)
    vop("sync", lambda e: e.dma_start(tmrv[:], tmr_in[:]), writes=["tmrv"],
        key="vtmr", inc=16)
    vop("sync", lambda e: e.dma_start(tdv[:], td_in[:]), writes=["tdv"],
        key="vtd", inc=16)
    vop("sync", lambda e: e.dma_start(tfv[:], tf_in[:]), writes=["tfv"],
        key="vtf", inc=16)
    vop("sync", lambda e: e.dma_start(wsv[:], ws_in[:]), writes=["wsv"],
        key="vws", inc=16)
    vop("gpsimd", lambda e: e.dma_start(biasbf[:], bias_in[:]),
        writes=["biasbf"], key="vbi", inc=16)
    FW8 = 8 * SP
    vop("gpsimd", lambda e: e.dma_start(wkbf[:], wk_in[:]), writes=["wkbf"],
        key="wwk", inc=16)
    vop("gpsimd", lambda e: e.dma_start(
        bass.AP(xbf[0], 1, [[FW8, 128], [SP, 8], [1, TC]]),
        bass.AP(xT, 0, [[T, 128], [T * 128, 8], [1, TC]])),
        writes=["xbf0"], key="xb0", inc=16)
    vop("gpsimd", lambda e: e.dma_start(wvbf[:], wv_in[:]), writes=["wvbf"],
        key="wwv", inc=16)
    vop("gpsimd", lambda e: e.dma_start(wrbf[:], wr_in[:]), writes=["wrbf"],
        key="wwr", inc=16)
    vop("gpsimd", lambda e: e.dma_start(
        bass.AP(xbf[1], 0, [[FW8, 128], [SP, 8], [1, TC + 1]]),
        bass.AP(xT, TC - 1, [[T, 128], [T * 128, 8], [1, TC + 1]])),
        writes=["xbf1"], key="xb1", inc=16)
    vop("gpsimd", lambda e: e.dma_start(wobf[:], wo_in[:]), writes=["wobf"],
        key="wwo", inc=16)
    vop("scalar", lambda e: e.activation(tia[:], tdv[:], ACT.Exp),
        reads=["tdv"], writes=["tia"])
    vop("vector", lambda e: e.tensor_scalar(tib[:], tia[:], -1.0, None,
                                            op0=ALU.mult),
        reads=["tia"], writes=["tib"])
    vop("scalar", lambda e: e.activation(ew[:], tib[:], ACT.Exp),
        reads=["tib"], writes=["ew"])
    vop("scalar", lambda e: e.activation(eu[:], tfv[:], ACT.Exp),
        reads=["tfv"], writes=["eu"])
    vop("vector", lambda e: e.tensor_scalar(ruu[:], tfv[:], 0.0, None,
                                            op0=ALU.max),
        reads=["tfv"], writes=["ruu"])
    vop("vector", lambda e: e.tensor_scalar(w2[:], wsv[:], 2.0 / 448.0,
                                            None, op0=ALU.mult),
        reads=["wsv"], writes=["w2"])
    rg0 = nc.gpsimd.alloc_register("rg0")
    rg1 = nc.gpsimd.alloc_register("rg1")
    vop("sync", lambda e: e.dma_start(hselt[:], hs_in[:]), writes=["hselt"],
        key="vhs", inc=16)
    hld = vop("gpsimd", lambda e: e.reg_load(rg0, hselt[0:1, 0:1]),
              reads=["hselt"])
    hld = vop("gpsimd", lambda e: e.reg_mul(rg0, rg0, 1024 * TC), extra=[hld])
    hld = vop("gpsimd", lambda e: e.reg_add(rg1, rg0, 1024 * TC), extra=[hld])
    bufs["hreg"].w = hld
    bufs["hreg2"] = Buf("hreg2")
    rgA = nc.gpsimd.alloc_register("rgA")
    rgB = nc.gpsimd.alloc_register("rgB")
    rgC = nc.gpsimd.alloc_register("rgC")
    rgD = nc.gpsimd.alloc_register("rgD")
    vop("gpsimd", lambda e: e.reg_load(rgA, hselt[0:1, 1:2]),
        reads=["hselt"], writes=["hregA"])
    vop("gpsimd", lambda e: e.reg_load(rgC, hselt[0:1, 2:3]),
        reads=["hselt"], writes=["hregC"])
    vop("gpsimd", lambda e: e.memset(ones_bf[:], 1.0), writes=["ones_bf"])
    vop("gpsimd", lambda e: e.affine_select(
        ident[:], ones_bf[:], pattern=[[1, 128]], compare_op=ALU.is_equal,
        fill=0.0, base=0, channel_multiplier=-1),
        reads=["ones_bf"], writes=["ident"])
    vop("gpsimd", lambda e: e.memset(mpr[0][:], -1e38), writes=["m0"])
    vop("gpsimd", lambda e: e.memset(tib[:, 0:1], -0.6931471805599453),
        reads=["ew"], writes=["tib"])
    vop("gpsimd", lambda e: e.memset(tib[:, 1:2], 1e-5),
        reads=["ew"], writes=["tib"])
    vop("gpsimd", lambda e: e.memset(cgv[:], 0.0), writes=["cgv"])
    vop("gpsimd", lambda e: e.memset(chv[:], 0.0), writes=["chv"])
    for ct in range(8):
        vop("gpsimd",
            lambda e, ct=ct: e.memset(xbf[0][:, ct * SP:ct * SP + 1], 0.0),
            writes=["xbf0"])
    for t1, t1n, tmt, tmn in ((tm1k, "tm1k", tmkv, "tmkv"),
                              (tm1v, "tm1v", tmvv, "tmvv"),
                              (tm1r, "tm1r", tmrv, "tmrv")):
        vop("vector", lambda e, t1=t1, tmt=tmt: e.tensor_scalar(
            t1[:], tmt[:], -1.0, 1.0, op0=ALU.mult, op1=ALU.add),
            reads=[tmn], writes=[t1n])
    for wsrc, wsn, tmt, tmn, t1, t1n, wd, wdn, wdb, wdbn in (
            (wkbf, "wkbf", tmkv, "tmkv", tm1k, "tm1k",
             wk2, "wk2", wk2b, "wk2b"),
            (wvbf, "wvbf", tmvv, "tmvv", tm1v, "tm1v",
             wv2, "wv2", wv2b, "wv2b"),
            (wrbf, "wrbf", tmrv, "tmrv", tm1r, "tm1r",
             wr2, "wr2", wr2b, "wr2b")):
        for ct in range(8):
            sl = slice(ct * 512, ct * 512 + 512)
            vop("scalar", lambda e, sl=sl, ct=ct, wd=wd, wsrc=wsrc, tmt=tmt:
                e.activation(wd[:, sl], wsrc[:, sl], ACT.Identity,
                             bias=0.0, scale=tmt[:, ct:ct + 1]),
                reads=[wsn, tmn], writes=[wdn])
            vop("vector", lambda e, sl=sl, ct=ct, wdb=wdb, wsrc=wsrc, t1=t1:
                e.tensor_tensor(wdb[:, sl], wsrc[:, sl],
                                bcol(t1, ct, 512, freew=8), op=ALU.mult),
                reads=[wsn, t1n], writes=[wdbn])
    braw4 = zb[:].bitcast(U8)[:, 0:2048]
    tebuf4 = zb[:].bitcast(U8)[:, 4096:6144]
    mtb4 = x2b[:].bitcast(U8)[:, 0:2048]
    mob4 = x2b[:].bitcast(U8)[:, 4096:6144]

    def emit_decode(q):
        sl = slice(q * 2048, q * 2048 + 2048)
        vop("sync", lambda e, sl=sl: e.dma_start(braw4, w8_in[:, sl]),
            writes=["zb"], key="w8", inc=16)
        vop("vector", lambda e: e.tensor_scalar(tebuf4, braw4, 0x78, None,
                                                op0=ALU.bitwise_and),
            reads=["zb"], writes=["zb"])
        vop("vector", lambda e: e.tensor_scalar(mtb4, tebuf4, 0x78, None,
                                                op0=ALU.is_equal),
            reads=["zb"], writes=["x2b"])
        vop("vector", lambda e: e.tensor_scalar(mob4, tebuf4, 0x78, None,
                                                op0=ALU.is_lt),
            reads=["zb"], writes=["x2b"])
        vop("vector", lambda e: e.scalar_tensor_tensor(
            tebuf4, braw4, 8, mtb4, op0=ALU.subtract, op1=ALU.mult),
            reads=["zb", "x2b"], writes=["zb"])
        vop("vector", lambda e: e.tensor_tensor(mtb4, braw4, mob4,
                                                op=ALU.mult),
            reads=["zb", "x2b"], writes=["x2b"])
        vop("vector", lambda e, sl=sl: e.scalar_tensor_tensor(
            w8bf[:, sl], tebuf4.bitcast(FP8), 2.0, mtb4.bitcast(FP8),
            op0=ALU.mult, op1=ALU.add),
            reads=["zb", "x2b"], writes=["w8bf"])

    def emit_front(j):
        slot = j % 2
        xb = xbf[slot]
        xbn = f"xbf{slot}"
        FW = 8 * SP
        if j == 0:
            src = bass.AP(xT, 0, [[T, 128], [T * 128, 8], [1, TC]])
            dst = bass.AP(xb, 1, [[FW, 128], [SP, 8], [1, TC]])
        else:
            src = bass.AP(xT, j * TC - 1, [[T, 128], [T * 128, 8],
                                           [1, TC + 1]])
            dst = bass.AP(xb, 0, [[FW, 128], [SP, 8], [1, TC + 1]])
        vop("gpsimd", lambda e, srcx=src, dstx=dst: e.dma_start(dstx, srcx),
            writes=[xbn], key=f"xb{slot}", inc=16)

    def mm_group(pbuf, pbn, emits, reads):
        tok = None
        for q, em in enumerate(emits):
            tok = vop("tensor", em, reads=reads,
                      writes=[pbn] if q == 0 else ())
        bufs[pbn].w = tok
        return tok

    def mm_group2(pbuf, pbn, emits):
        tok = None
        for q, (em, rd) in enumerate(emits):
            tok = vop("tensor", em, reads=rd,
                      writes=[pbn] if q == 0 else ())
        bufs[pbn].w = tok
        return tok

    def proj_ems(pb, wd, wdb, xb, dt):
        ems = []
        for ct in range(8):
            ems.append(lambda e, ct=ct, pb=pb, wd=wd, xb=xb, dt=dt: e.matmul(
                pb, wd[:, ct * 512 + dt * 128:ct * 512 + dt * 128 + 128],
                xb[:, ct * SP + 1:ct * SP + 1 + TC],
                start=(ct == 0), stop=False))
            ems.append(lambda e, ct=ct, pb=pb, wdb=wdb, xb=xb, dt=dt:
                       e.matmul(
                pb, wdb[:, ct * 512 + dt * 128:ct * 512 + dt * 128 + 128],
                xb[:, ct * SP:ct * SP + TC],
                start=False, stop=(ct == 7)))
        return ems

    def emit_back_a(j, mid=None):
        slot = j % 2
        xb = xbf[slot]
        xbn = f"xbf{slot}"
        mp, mn = mpr[j % 2], mpr[(j + 1) % 2]
        mpn, mnn = f"m{j % 2}", f"m{(j + 1) % 2}"
        cstep = TC + 1
        kbank = [[(pk[0][:], "pk0"), (pk[1][:], "pk1")],
                 [(pw[0][:, 0:512], "pw0"), (pw[0][:, 512:1024], "pw0")]]
        vbank = [[(pv[0][:], "pv0"), (pv[1][:], "pv1")],
                 [(pw[1][:, 0:512], "pw1"), (pw[1][:, 512:1024], "pw1")]]
        for half in range(2):
            h2 = slice(2 * half, 2 * half + 2)
            for sub in range(2):
                dt = 2 * half + sub
                kb, kbn = kbank[half][sub]
                mm_group(kb, kbn,
                         proj_ems(kb, wk2, wk2b, xb, dt),
                         ["wk2", "wk2b", xbn])
            for sub in range(2):
                dt = 2 * half + sub
                vb, vbn = vbank[half][sub]
                mm_group(vb, vbn,
                         proj_ems(vb, wv2, wv2b, xb, dt),
                         ["wv2", "wv2b", xbn])
            for sub in range(2):
                dt = 2 * half + sub
                kb, kbn = kbank[half][sub]
                vop("vector", lambda e, dt=dt, kb=kb: e.reduce_max(
                    kmx[:, dt:dt + 1], kb, axis=AX),
                    reads=[kbn], writes=["kmx"])
            vop("vector", lambda e, h2=h2: e.tensor_add(
                ta4[:, h2], kmx[:, h2], ruu[:, h2]),
                reads=["kmx", "ruu"], writes=["ta4"])
            vop("vector", lambda e, h2=h2, mp=mp, mn=mn: e.tensor_max(
                mn[:, h2], mp[:, h2], ta4[:, h2]),
                reads=[mpn, "ta4"], writes=[mnn])
            vop("vector", lambda e, h2=h2, mn=mn: e.tensor_scalar(
                negm[:, h2], mn[:, h2], -1.0, None, op0=ALU.mult),
                reads=[mnn], writes=["negm"])
            vop("vector", lambda e, h2=h2, mp=mp, mn=mn: e.tensor_sub(
                dm4[:, h2], mp[:, h2], mn[:, h2]),
                reads=[mpn, mnn], writes=["dm4"])
            vop("scalar", lambda e, h2=h2: e.activation(
                rsc[:, h2], dm4[:, h2], ACT.Exp),
                reads=["dm4"], writes=["rsc"])
            vop("vector", lambda e, h2=h2: e.tensor_mul(
                cig[:, h2], cgv[:, h2], rsc[:, h2]),
                reads=["cgv", "rsc"], writes=["cig"])
            vop("vector", lambda e, h2=h2: e.tensor_mul(
                cih[:, h2], chv[:, h2], rsc[:, h2]),
                reads=["chv", "rsc"], writes=["cih"])
            for sub in range(2):
                dt = 2 * half + sub
                kb, kbn = kbank[half][sub]
                vb, vbn = vbank[half][sub]
                vop("scalar", lambda e, dt=dt, kb=kb: e.activation(
                    Ab[:, dt * 512:(dt + 1) * 512], kb, ACT.Exp,
                    bias=negm[:, dt:dt + 1], scale=1.0),
                    reads=[kbn, "negm"], writes=["Ab"])
                vop("vector", lambda e, dt=dt, vb=vb: e.tensor_mul(
                    Bb[:, dt * 512:(dt + 1) * 512],
                    Ab[:, dt * 512:(dt + 1) * 512], vb),
                    reads=["Ab", vbn], writes=["Bb"])
        rbank = [pk[0][:], pk[1][:], pv[0][:], pv[1][:]]
        rbn = ["pk0", "pk1", "pv0", "pv1"]
        for dt in range(4):
            mm_group(rbank[dt], rbn[dt],
                     proj_ems(rbank[dt], wr2, wr2b, xb, dt),
                     ["wr2", "wr2b", xbn])
            vop("scalar", lambda e, dt=dt: e.activation(
                rwbt[:, dt * 512:(dt + 1) * 512], rbank[dt], ACT.Tanh,
                bias=0.0, scale=0.5),
                reads=[rbn[dt]], writes=["rwbt"])
        if mid is not None:
            mid()
        vop("vector", lambda e: e.tensor_copy(
            bass.AP(Cb, 0, [[4 * cstep, 128], [cstep, 4], [1, 1]]),
            bass.AP(cig, 0, [[4, 128], [1, 4], [1, 1]])),
            reads=["cig"], writes=["Cb"])
        vop("vector", lambda e: e.tensor_copy(
            bass.AP(Db, 0, [[4 * cstep, 128], [cstep, 4], [1, 1]]),
            bass.AP(cih, 0, [[4, 128], [1, 4], [1, 1]])),
            reads=["cih"], writes=["Db"])
        for dt in range(4):
            base = dt * cstep
            vop("vector", lambda e, dt=dt, base=base: e.tensor_tensor_scan(
                Cb[:, base + 1:base + 1 + TC], bcol(ew, dt, TC),
                Bb[:, dt * 512:(dt + 1) * 512], cig[:, dt:dt + 1],
                op0=ALU.mult, op1=ALU.add),
                reads=["ew", "Bb", "cig"], writes=["Cb"])
            vop("vector", lambda e, dt=dt, base=base: e.tensor_tensor_scan(
                Db[:, base + 1:base + 1 + TC], bcol(ew, dt, TC),
                Ab[:, dt * 512:(dt + 1) * 512], cih[:, dt:dt + 1],
                op0=ALU.mult, op1=ALU.add),
                reads=["ew", "Ab", "cih"], writes=["Db"])
        for dt in range(4):
            base = dt * cstep
            vop("vector", lambda e, dt=dt, base=base: e.scalar_tensor_tensor(
                E1[:, dt * 512:(dt + 1) * 512],
                Bb[:, dt * 512:(dt + 1) * 512], eu[:, dt:dt + 1],
                Cb[:, base:base + TC], op0=ALU.mult, op1=ALU.add),
                reads=["Bb", "eu", "Cb"], writes=["E1"])
        vop("vector", lambda e: e.tensor_copy(
            bass.AP(cgv, 0, [[4, 128], [1, 4], [1, 1]]),
            bass.AP(Cb, TC, [[4 * cstep, 128], [cstep, 4], [1, 1]])),
            reads=["Cb"], writes=["cgv"])
        vop("vector", lambda e: e.tensor_copy(
            bass.AP(chv, 0, [[4, 128], [1, 4], [1, 1]]),
            bass.AP(Db, TC, [[4 * cstep, 128], [cstep, 4], [1, 1]])),
            reads=["Db"], writes=["chv"])
        for dt in range(4):
            base = dt * cstep
            vop("vector", lambda e, dt=dt, base=base:
                e.scalar_tensor_tensor(
                    Cb[:, base:base + TC],
                    Ab[:, dt * 512:(dt + 1) * 512], eu[:, dt:dt + 1],
                    Db[:, base:base + TC], op0=ALU.mult, op1=ALU.add),
                reads=["Ab", "eu", "Db", "cgv"], writes=["Cb"])
        for dt in range(4):
            base = dt * cstep
            vop("scalar", lambda e, base=base: e.activation(
                lnscr[:], Cb[:, base:base + TC], ACT.Ln),
                reads=["Cb"], writes=["lnscr"])
            vop("scalar", lambda e, dt=dt: e.activation(
                Bb[:, dt * 512:(dt + 1) * 512], lnscr[:], ACT.Exp,
                bias=tib[:, 0:1], scale=-1.0),
                reads=["lnscr", "tib"], writes=["Bb"])

    def emit_back_b(j):
        cstep = TC + 1
        wkv3 = bass.AP(Cb, 0, [[4 * cstep, 128], [cstep, 4], [1, TC]])
        e13 = bass.AP(E1, 0, [[2048, 128], [512, 4], [1, TC]])
        bb3 = bass.AP(Bb, 0, [[2048, 128], [512, 4], [1, TC]])
        vop("vector", lambda e, o=wkv3, a=e13, bq=bb3: e.tensor_tensor(
            o, a, bq, op=ALU.mult),
            reads=["E1", "Bb"], writes=["Cb"])
        vop("vector", lambda e, w=wkv3: e.scalar_tensor_tensor(
            srb, rwbt[:], 1.0, w, op0=ALU.add, op1=ALU.mult),
            reads=["rwbt", "Cb"], writes=["Db"])
        src2 = bass.AP(Db.bitcast(BF16), 2048, [[4104, 128], [512, 4],
                                                [1, 512]])
        dst2 = bass.AP(cc_big, j * 512 * TC,
                       [[TC, 128], [128 * TC, 4], [1, TC]])
        vop("sync", lambda e, a=dst2, bq=src2: e.dma_start(a, bq),
            reads=["Db"], writes=[f"cc_in{j}"], key=f"cci{j}", inc=16)

    groups = [[0, 1], [2, 3], [4, 5], [6, 7]]

    def emit_cc(j):
        in_ap = bass.AP(cc_big, j * 512 * TC, [[TC, 512], [1, TC]])
        out_ap = bass.AP(cc_out_all, j * 1024 * TC, [[TC, 1024], [1, TC]])
        vop("gpsimd", lambda e, in_ap=in_ap, out_ap=out_ap:
            e.collective_compute(
                "AllGather", ALU.bypass, replica_groups=groups,
                ins=[in_ap.opt()], outs=[out_ap.opt()]),
            reads=[f"cc_in{j}"], writes=[f"cc_out{j}"], key=f"cc{j}", inc=1)

    def emit_phase2(i):
        def sub_a(hh):
            if hh == 0:
                vop("gpsimd", lambda e, base=2 * i * 512 * TC:
                    e.reg_add(rgB, rgA, base),
                    reads=["hregA"], writes=["hregB"])
                dstL = bass.AP(rwg, 0, [[4096, 128], [512, 4], [1, TC]])
                vop("gpsimd", lambda e, a=dstL: e.dma_start(
                    a, bass.AP(cc_big, rgB,
                               [[TC, 128], [128 * TC, 4], [1, TC]])),
                    reads=["hregB"], writes=["rwgL"],
                    extra=[bufs[f"cc_in{2 * i}"].w,
                           bufs[f"cc_in{2 * i + 1}"].w],
                    key="rwgL", inc=16)
                vop("gpsimd", lambda e, base=2 * i * 1024 * TC:
                    e.reg_add(rgD, rgC, base),
                    reads=["hregC"], writes=["hregD"])
                dstR = bass.AP(rwg, 2048, [[4096, 128], [512, 4], [1, TC]])
                vop("gpsimd", lambda e, a=dstR: e.dma_start(
                    a, bass.AP(cc_out_all, rgD,
                               [[TC, 128], [128 * TC, 4], [1, TC]])),
                    reads=["hregD"], writes=["rwgR"],
                    extra=[bufs[f"cc_out{2 * i}"].w,
                           bufs[f"cc_out{2 * i + 1}"].w],
                    key="rwgR", inc=16)
            row0 = 512 * i + 256 * hh
            srct = bass.AP(x_tm, row0 * C, [[C, 128], [128 * C, 2], [1, C]])
            dstt = bass.AP(xtmb, 0, [[2 * C, 128], [C, 2], [1, C]])
            vop("sync", lambda e, a=dstt, bq=srct: e.dma_start(a, bq),
                writes=["xtmb"], key="xtm", inc=16)
            for tt in range(2):
                tbase = hh * 256 + tt * 128
                ems = []
                for dt in list(range(4)) + list(range(4, 8)):
                    for nn in range(2):
                        ems.append((lambda e, nn=nn, dt=dt, pb=pw[tt],
                                    tbase=tbase: e.matmul(
                            pb[:, nn * 512:(nn + 1) * 512],
                            rwg[:, dt * 512 + tbase:dt * 512 + tbase + 128],
                            wobf[:, dt * 1024 + nn * 512:
                                 dt * 1024 + (nn + 1) * 512],
                            start=(dt == 0), stop=(dt == 7)),
                            ["rwgL" if dt < 4 else "rwgR", "wobf"]))
                mm_group2(pw[tt], f"pw{tt}", ems)
            return []

        def sub_b(hh):
            row0 = 512 * i + 256 * hh
            for tt in range(2):
                vop("vector", lambda e, tt=tt: e.scalar_tensor_tensor(
                    zb[:, tt * C:(tt + 1) * C], pw[tt][:], 1.0,
                    xtmb[:, tt * C:(tt + 1) * C], op0=ALU.mult, op1=ALU.add,
                    accum_out=s1v[:, tt:tt + 1]),
                    reads=[f"pw{tt}", "xtmb"], writes=["zb", "s1v"])
            for tt in range(2):
                vop("scalar", lambda e, tt=tt: e.activation(
                    qbfb[:].bitcast(F32)[:, 0:C],
                    zb[:, tt * C:(tt + 1) * C], ACT.Square,
                    accum_out=s2v[:, tt:tt + 1]),
                    reads=["zb"], writes=["qbfb", "s2v"])
            ln_stats()
            for tt in range(2):
                vop("scalar", lambda e, tt=tt: e.activation(
                    x2b[:, tt * C:(tt + 1) * C], zb[:, tt * C:(tt + 1) * C],
                    ACT.Identity, bias=ngm[:, tt:tt + 1],
                    scale=istd[:, tt:tt + 1]),
                    reads=["zb", "ngm", "istd"], writes=["x2b"])
            for tt in range(2):
                vop("vector", lambda e, tt=tt: e.reduce_max(
                    amx[:, tt:tt + 1], x2b[:, tt * C:(tt + 1) * C], axis=AX,
                    apply_absolute_value=True),
                    reads=["x2b"], writes=["amx"])
            vop("vector", lambda e: e.tensor_scalar(
                stv[:], amx[:], 1e-12, w2[:, 0:1], op0=ALU.max,
                op1=ALU.mult),
                reads=["amx", "w2"], writes=["stv"])
            vop("vector", lambda e: e.reciprocal(invs[:], stv[:]),
                reads=["stv"], writes=["invs"])
            vop("vector", lambda e: e.tensor_scalar(
                i2x[:], invs[:], wsv[:, 0:1], None, op0=ALU.mult),
                reads=["invs", "wsv"], writes=["i2x"])
            vop("scalar", lambda e: e.activation(invc[:], invs[:], ACT.Copy),
                reads=["invs"], writes=["invc"])
            for tt in range(2):
                vop("scalar", lambda e, tt=tt: e.activation(
                    q8b[:, tt * C:(tt + 1) * C], x2b[:, tt * C:(tt + 1) * C],
                    ACT.Copy, bias=0.0, scale=i2x[:, tt:tt + 1]),
                    reads=["x2b", "i2x"], writes=["qfmb"])
            vop("scalar", lambda e: e.activation(qbfb[:], q8b, ACT.Copy),
                reads=["qfmb"], writes=["qbfb"])
            return []

        def sub_c(hh):
            for tt in range(2):
                pb, pbn = pw[tt], f"pw{tt}"
                tok = None
                for bqi in range(8):
                    tok = vop("tensor", lambda e, tt=tt, bqi=bqi, pb=pb:
                              e.transpose(
                                  pb[:].bitcast(BF16)[:, bqi * 128:
                                                      (bqi + 1) * 128],
                                  qbfb[:, tt * C + bqi * 128:
                                       tt * C + (bqi + 1) * 128],
                                  ident[:]),
                              reads=["qbfb", "ident"],
                              writes=[pbn] if bqi == 0 else ())
                bufs[pbn].w = tok
                tok = vop("tensor", lambda e, tt=tt, pb=pb: e.transpose(
                    pb[:].bitcast(BF16)[0:1, 1024:1152],
                    invc[:, tt:tt + 1], ident[:]),
                    reads=["invc", "ident"])
                bufs[pbn].w = tok
                vop("scalar", lambda e, tt=tt, pb=pb: e.activation(
                    bass.AP(qfmb, tt * 128, [[2048, 128], [256, 8], [1, 128]]),
                    bass.AP(pb.bitcast(BF16), 0,
                            [[2048, 128], [128, 8], [1, 128]]),
                    ACT.Copy),
                    reads=[pbn], writes=["qfmb"])
                vop("scalar", lambda e, tt=tt, pb=pb: e.activation(
                    qinv[0:1, tt * 128:(tt + 1) * 128],
                    pb[:].bitcast(BF16)[0:1, 1024:1152], ACT.Copy),
                    reads=[pbn], writes=["qbfb"])
            for tt in range(2):
                for nn in range(2):
                    pb = fbank[tt][nn]
                    ems = []
                    for kt in range(8):
                        ems.append((lambda e, nn=nn, kt=kt, tt=tt, pb=pb:
                                    e.matmul(
                            pb[:],
                            qfmb[:, kt * 256 + tt * 128:
                                 kt * 256 + tt * 128 + 128],
                            w8bf[:, kt * 1024 + nn * 512:
                                 kt * 1024 + (nn + 1) * 512],
                            start=(kt == 0), stop=False),
                            ["qfmb", "w8bf"]))
                    ems.append((lambda e, nn=nn, tt=tt, pb=pb: e.matmul(
                        pb[:],
                        qinv[0:1, tt * 128:(tt + 1) * 128],
                        biasbf[0:1, nn * 512:(nn + 1) * 512],
                        start=False, stop=True),
                        ["qbfb", "biasbf"]))
                    mm_group2(pb, fbn[tt][nn], ems)
            return []

        def sub_d(hh):
            row0 = 512 * i + 256 * hh
            for tt in range(2):
                for nn in range(2):
                    sl = slice(tt * C + nn * 512, tt * C + nn * 512 + 512)
                    vop("vector", lambda e, tt=tt, nn=nn, sl=sl:
                        e.scalar_tensor_tensor(
                            zb[:, sl], fbank[tt][nn][:],
                            stv[:, tt:tt + 1], x2b[:, sl],
                            op0=ALU.mult, op1=ALU.add,
                            accum_out=s1v4[:, 2 * tt + nn:2 * tt + nn + 1]),
                        reads=[fbn[tt][nn], "stv", "x2b"],
                        writes=["zb", "s1v4"])
            vop("vector", lambda e: e.tensor_tensor(
                s1v[:], bass.AP(s1v4, 0, [[4, 128], [2, 2]]),
                bass.AP(s1v4, 1, [[4, 128], [2, 2]]), op=ALU.add),
                reads=["s1v4"], writes=["s1v"])
            for tt in range(2):
                vop("scalar", lambda e, tt=tt: e.activation(
                    qbfb[:].bitcast(F32)[:, 0:C],
                    zb[:, tt * C:(tt + 1) * C], ACT.Square,
                    accum_out=s2v[:, tt:tt + 1]),
                    reads=["zb"], writes=["qbfb", "s2v"])
            ln_stats()
            for tt in range(2):
                vop("scalar", lambda e, tt=tt: e.activation(
                    x2b[:, tt * C:(tt + 1) * C], zb[:, tt * C:(tt + 1) * C],
                    ACT.Identity, bias=ngm[:, tt:tt + 1],
                    scale=istd[:, tt:tt + 1]),
                    reads=["zb", "ngm", "istd"], writes=["x2b"])
            srcy = bass.AP(x2b, 0, [[2 * C, 128], [C, 2], [1, C]])
            dsty = bass.AP(y_out, row0 * C, [[C, 128], [128 * C, 2], [1, C]])
            return [vop("sync", lambda e, a=dsty, bq=srcy: e.dma_start(a, bq),
                        reads=["x2b"], key="yout", inc=16)]

        def ln_stats():
            vop("vector", lambda e: e.tensor_scalar(
                mea[:], s1v[:], 1.0 / C, None, op0=ALU.mult),
                reads=["s1v"], writes=["mea"])
            vop("vector", lambda e: e.tensor_mul(msq[:], mea[:], mea[:]),
                reads=["mea"], writes=["msq"])
            vop("vector", lambda e: e.scalar_tensor_tensor(
                lnv[:], s2v[:], 1.0 / C, msq[:], op0=ALU.mult,
                op1=ALU.subtract),
                reads=["s2v", "msq"], writes=["lnv"])
            vop("scalar", lambda e: e.activation(
                varv[:], lnv[:], ACT.Ln, bias=tib[:, 1:2], scale=1.0),
                reads=["lnv", "tib"], writes=["varv"])
            vop("scalar", lambda e: e.activation(
                istd[:], varv[:], ACT.Exp, bias=0.0, scale=-0.5),
                reads=["varv"], writes=["istd"])
            vop("vector", lambda e: e.scalar_tensor_tensor(
                ngm[:], mea[:], -1.0, istd[:], op0=ALU.mult, op1=ALU.mult),
                reads=["mea", "istd"], writes=["ngm"])

        ab, cd = [], []
        for hh in range(2):
            ab.append((lambda hh=hh: sub_a(hh), lambda hh=hh: sub_b(hh)))
            cd.append((lambda hh=hh: sub_c(hh), lambda hh=hh: sub_d(hh)))
        return ab, cd

    out_toks = []
    abq, cdq = [], []
    ab_count = [0]
    cd_count = [0]
    pend_b = []
    pend_d = []

    def pop_c(j):
        if cdq and j >= cd_count[0] + 3:
            c, d = cdq.pop(0)
            c()
            pend_d.append(d)
            cd_count[0] += 1

    def pop_d():
        while pend_d:
            out_toks.extend(pend_d.pop(0)())

    def pop_a(j):
        if abq and j >= ab_count[0] + 2:
            a, b = abq.pop(0)
            a()
            pend_b.append(b)
            ab_count[0] += 1

    def pop_b():
        while pend_b:
            pend_b.pop(0)()

    for j in range(NCH):
        if 2 <= j + 1 < NCH:
            emit_front(j + 1)
        if j % 2 == 1:
            ab, cd = emit_phase2(j // 2)
            abq.extend(ab)
            cdq.extend(cd)
        emit_back_a(j, mid=lambda jj=j: (pop_c(jj), pop_a(jj)))
        if j < 2:
            emit_decode(j)
        if j == 2:
            emit_decode(2)
            emit_decode(3)
        pop_d()
        emit_back_b(j)
        emit_cc(j)
        pop_b()
    j = NCH
    while abq or cdq or pend_b or pend_d:
        pop_c(j)
        pop_a(j)
        pop_d()
        pop_b()
        j += 1

    s.op("gpsimd", lambda e: e.engine_nop(), deps=out_toks)

    s.emit()
    nc.finalize()
    return nc


_NC_CACHE = [None]


def kernel(L0_attn_x, L0_attn_time_decay, L0_attn_time_first,
           L0_attn_time_mix_k, L0_attn_time_mix_v, L0_attn_time_mix_r,
           L0_attn_Wk, L0_attn_Wv, L0_attn_Wr, L0_attn_Wo,
           L0_ffn_weight, L0_ffn_weight_scale, L0_ffn_bias):
    x = np.ascontiguousarray(np.asarray(L0_attn_x, dtype=np.float32))

    def vec128(v):
        v = np.ascontiguousarray(np.asarray(v, dtype=np.float32))
        return np.ascontiguousarray(v.reshape(-1, 128).T)

    def wlay(wt, cols):
        r = wt.shape[0]
        return np.ascontiguousarray(
            wt.reshape(r // 128, 128, cols).transpose(1, 0, 2)
            .reshape(128, (r // 128) * cols))

    wkT = np.asarray(L0_attn_Wk, dtype=np.float32).T
    wvT = np.asarray(L0_attn_Wv, dtype=np.float32).T
    wrT = np.asarray(L0_attn_Wr, dtype=np.float32).T
    woT = np.asarray(L0_attn_Wo, dtype=np.float32).T
    w8T = np.ascontiguousarray(
        np.asarray(L0_ffn_weight).view(np.uint8).T)
    ws = np.float32(np.asarray(L0_ffn_weight_scale, dtype=np.float32))
    bias = np.ascontiguousarray(
        np.asarray(L0_ffn_bias, dtype=np.float32)[None, :])

    tmk = vec128(L0_attn_time_mix_k)
    tmv = vec128(L0_attn_time_mix_v)
    tmr = vec128(L0_attn_time_mix_r)
    td = np.asarray(L0_attn_time_decay, dtype=np.float32)
    tf = np.asarray(L0_attn_time_first, dtype=np.float32)
    wo_l = wlay(np.ascontiguousarray(woT), 1024)
    wo8 = wo_l.reshape(128, 8, 1024)
    wo_swap = np.ascontiguousarray(
        np.concatenate([wo8[:, 4:], wo8[:, :4]], axis=1).reshape(128, 8192))
    w8_l = wlay(w8T, 1024)

    if _NC_CACHE[0] is None:
        _NC_CACHE[0] = build_nc()
    nc = _NC_CACHE[0]

    in_maps = []
    for core in range(8):
        b, h = core // 2, core % 2
        H = slice(512 * h, 512 * h + 512)
        xTb = np.ascontiguousarray(x[b].T)
        rows = [x[b, 512 * j:512 * j + 512] for j in range(h, 8, 2)]
        xtm = np.ascontiguousarray(np.concatenate(rows, axis=0))
        in_maps.append({
            "xT": xTb,
            "x_tm": xtm,
            "wk_in": wlay(np.ascontiguousarray(wkT[:, H]), 512),
            "wv_in": wlay(np.ascontiguousarray(wvT[:, H]), 512),
            "wr_in": wlay(np.ascontiguousarray(wrT[:, H]), 512),
            "wo_in": wo_l if h == 0 else wo_swap,
            "w8_in": w8_l,
            "tmk_in": tmk, "tmv_in": tmv, "tmr_in": tmr,
            "td_in": vec128(td[H]), "tf_in": vec128(tf[H]),
            "ws_in": np.ascontiguousarray(
                np.broadcast_to(ws, (128, 1)).astype(np.float32)),
            "bias_in": bias,
            "hs_in": np.array(
                [[h, h * 512 * TC,
                  h * 1024 * TC + (1 - h) * 512 * TC]], dtype=np.int32),
        })

    res = run_bass_kernel_spmd(nc, in_maps, core_ids=list(range(8)))
    out = np.empty((B, T, C), dtype=np.float32)
    for core in range(8):
        b, h = core // 2, core % 2
        yo = np.asarray(res.results[core]["y_out"])
        for idx, j in enumerate(range(h, 8, 2)):
            out[b, 512 * j:512 * j + 512] = yo[512 * idx:512 * idx + 512]
    return out
